# version 01
# baseline (speedup 1.0000x reference)
import numpy as np
import contextlib
import concourse.bass as bass
import concourse.mybir as mybir
from concourse.bass_utils import run_bass_kernel_spmd

F32 = mybir.dt.float32
BF16 = mybir.dt.bfloat16
AF = mybir.ActivationFunctionType
ALU = mybir.AluOpType
AX = mybir.AxisListType

NCORES = 8
B_CORE = 4
L = 2048
H = 1024
NH = 8
HID2 = 2048
NE = 8
EHID = 2048
G = 512
NGRP = B_CORE * L // G
GPS = L // G
CPG = G // 128
TOK = B_CORE * L
LN_EPS = 1e-5
RMS_EPS = 1e-5
FFN_EPS = 1e-6
L2_EPS = 1e-6
NI = 2


class Prog:
    def __init__(self):
        self.ops = []
        self.names = []
        self.counts = []

    def sem(self, name):
        i = len(self.names)
        self.names.append(name)
        self.counts.append(0)
        return i

    def now(self, s):
        return (s, self.counts[s])

    def op(self, eng, fn, waits=(), inc=None, dma=False):
        incs = []
        if inc is not None:
            for s in ([inc] if isinstance(inc, int) else inc):
                d = 16 if dma else 1
                self.counts[s] += d
                incs.append((s, d))
        self.ops.append((eng, fn, [w for w in waits if w is not None], incs))
        if incs:
            return (incs[0][0], self.counts[incs[0][0]])
        return None

    def replay(self, eng_name, eng, sems):
        hw = {}
        for (e, fn, waits, incs) in self.ops:
            if e != eng_name:
                continue
            best = {}
            for (s, v) in waits:
                if v > 0 and v > hw.get(s, 0) and v > best.get(s, 0):
                    best[s] = v
            for s, v in best.items():
                eng.wait_ge(sems[s], v)
                hw[s] = v
            inst = fn(eng)
            for (s, d) in incs:
                inst.then_inc(sems[s], d)


def ts(i, n=128):
    return slice(i * n, (i + 1) * n)


def build(upto="moe", dbg_keys=(), dbg_group=0):
    nc = bass.Bass("TRN2", target_bir_lowering=False)
    p = Prog()

    dx = nc.dram_tensor("x", [TOK, H], F32, kind="ExternalInput").ap()
    dout = nc.dram_tensor("out", [TOK, H], F32, kind="ExternalOutput").ap()
    dW = {}
    for nm, shp in [("Wq", [H, H]), ("Wk", [H, H]), ("Wv", [H, H]),
                    ("Wo", [H, H]), ("fus_W1", [H, HID2]),
                    ("fus_W2", [HID2, 32]), ("Wb", [H, NH]),
                    ("floor_W", [H, NH]), ("w_routerT", [H, NE]),
                    ("w_upT", [NE, H, 2 * EHID]), ("w_downT", [NE, EHID, H]),
                    ("conv_q_pt", [128, 32]), ("conv_k_pt", [128, 32]),
                    ("conv_v_pt", [128, 32]), ("fir_s_pt", [128, 24]),
                    ("fir_l_pt", [128, 248]), ("fus_b1_pt", [128, 16]),
                    ("fus_b2_c", [32, 1]), ("floor_b_c", [NH, 1]),
                    ("norm_pf_r", [1, 128]), ("norm_o_r", [1, 128]),
                    ("gamma_r", [1, H])]:
        dW[nm] = nc.dram_tensor(nm, shp, F32, kind="ExternalInput").ap()
    scr_up = nc.dram_tensor("scr_up", [NE, 128, 8, 2 * EHID], BF16,
                            kind="Internal").ap()
    scr_dn = nc.dram_tensor("scr_dn", [NE, 128, 16, H], BF16,
                            kind="Internal").ap()

    sX = p.sem("x_dma")
    sWA = p.sem("w_dma")
    sOS = p.sem("out_store")
    sPE = p.sem("pe")
    sDV = p.sem("dve")
    sAC = p.sem("act")
    sPO = p.sem("pool")

    stack = contextlib.ExitStack()
    sb = lambda name, shape, dt: stack.enter_context(nc.sbuf_tensor(name, shape, dt))
    psa = lambda name, shape, dt: stack.enter_context(nc.psum_tensor(name, shape, dt))

    with stack:
        c_cw = {"Wq": sb("cwq", [128, 32], F32), "Wk": sb("cwk", [128, 32], F32),
                "Wv": sb("cwv", [128, 32], F32)}
        c_fs = sb("firs", [128, 24], F32)
        c_fl = sb("firl", [128, 248], F32)
        c_b1 = sb("fusb1", [128, 16], F32)
        c_b2 = sb("fusb2", [32, 1], F32)
        c_fb = sb("floorb", [NH, 1], F32)
        c_ones = sb("onesrow", [1, 128], F32)
        c_id16 = sb("id16", [128, 128], BF16)
        c_idf = sb("idf", [128, 128], F32)
        c_mslu = sb("mslu", [128, 2, 128], F32)
        c_miu = sb("incl_up", [128, 128], F32)
        c_pfb = sb("pf_b", [128, 128], F32)
        c_nob = sb("no_b", [128, 128], F32)
        c_gmb = sb("gm_b", [128, H], F32)
        r_wb = sb("r_wb", [128, 8, NH], BF16)
        r_wf = sb("r_wf", [128, 8, NH], BF16)
        r_wr = sb("r_wr", [128, 8, NE], F32)
        r_w2 = sb("r_w2", [128, 16, 32], BF16)

        x_t = sb("x_t", [128, 1, H], F32)
        xn_tok = sb("xn_tok", [128, 1, H], F32)
        xr_tok = sb("xr_tok", [128, 1, H], F32)
        dump = sb("dump", [128, H], F32)
        stat = sb("stat", [128, 2, 16], F32)
        ext = sb("ext", [128, 64], F32)
        outp = sb("outp", [128, 2, H], F32)
        xn_fm = sb("xn_fm", [128, 8, G], BF16)
        xr_fm32 = sb("xr_fm32", [128, 8, 128], F32)
        xr_fm = sb("xr_fm", [128, 8, G], BF16)
        w8 = sb("w8", [128, CPG, NE], F32)
        beta_b = sb("beta_b", [128, CPG, NH], F32)
        coef = sb("coef", [128, CPG, 4, NH], F32)
        sc_fm = sb("sc_fm", [NE, 128], F32)
        bf_fm = sb("bf_fm", [NH, 2, G], BF16)
        fus_fm = sb("fus_fm", [32, G], F32)
        fus_tok = sb("fus_tok", [128, 32], F32)

        raw = sb("raw", [128, 8, 4 + G], BF16)
        tails = sb("tails", [128, 3, 8, 4], BF16)
        cacc = sb("cacc", [128, 2, G], F32)
        arena = sb("arena", [128, 16384], mybir.dt.uint8)
        q_fm = arena.ap()[:, 0:8192].bitcast(BF16).rearrange(
            "p (a b) -> p a b", a=8)
        k_fm = arena.ap()[:, 8192:16384].bitcast(BF16).rearrange(
            "p (a b) -> p a b", a=8)
        acc = arena.ap()[:, :].bitcast(F32).rearrange(
            "p (a b) -> p a b", a=CPG)
        v_fir = sb("v_fir", [128, 8, 32 + G], BF16)
        hid = sb("hid", [128, 16, G], BF16)
        fir_s = sb("fir_s", [128, 8, G], BF16)
        fir_l = sb("fir_l", [128, 8, G], BF16)

        slot = [sb(f"slot{i}", [128, 8, 1024], BF16) for i in range(2)]
        wd_sl = sb("wdsl", [128, 16, 512], BF16)

        S32 = sb("S32", [128, NH, 128], F32)
        S16 = sb("S16", [128, NH, 128], BF16)
        dstat = sb("dstat", [128, NI, 16], F32)
        d_tok = sb("d_tok", [128, NI, 3, 128], BF16)
        d_fm3 = sb("d_fm3", [128, NI, 3, 128], BF16)
        d_knt = sb("d_knt", [128, NI, 128], BF16)
        d_kbt = sb("d_kbt", [128, NI, 128], BF16)
        d_vbt = sb("d_vbt", [128, NI, 128], BF16)
        d_AAT = sb("d_AAT", [128, NI, 2, 128], BF16)
        d_PP = sb("d_PP", [128, NI, 5, 2, 128], BF16)
        d_T = sb("d_T", [128, NI, 2, 128], BF16)
        d_uw = sb("d_uw", [128, NI, 2, 128], BF16)
        d_alT = sb("d_alT", [128, NI, 128], BF16)
        d_unew = sb("d_unew", [128, NI, 128], BF16)
        d_o2 = sb("d_o2", [128, NI, 128], BF16)
        d_ft = sb("d_ft", [128, NI, 2, 128], BF16)
        o_tmp = sb("o_tmp", [128, NI, 128], F32)
        o1_tmp = sb("o1_tmp", [128, NI, 128], F32)
        o2_fm = sb("o2_fm", [128, 8, 128], BF16)
        sg_ = sb("sg", [128, 2, 512], BF16)

        pb = [psa(f"pb{i}", [128, 512], F32) for i in range(6)]
        pt16 = [psa(f"pt{i}", [128, 1024], BF16) for i in range(2)]

        ctx = {}

        def bankw(key):
            return ctx.get(("bk", key))

        def bankr(key, ev):
            ctx[("bk", key)] = ev

        def ld(dst, src, waits=()):
            return p.op("sync", lambda e, d=dst, s=src: e.dma_start(out=d, in_=s),
                        waits=waits, inc=sX, dma=True)

        def ldw(dst, src, waits=()):
            return p.op("pool", lambda e, d=dst, s=src: e.dma_start(out=d, in_=s),
                        waits=waits, inc=sWA, dma=True)

        def st_dma(dst, src, waits=()):
            return p.op("sync", lambda e, d=dst, s=src: e.dma_start(out=d, in_=s),
                        waits=waits, inc=sOS, dma=True)

        ld(c_cw["Wq"][:, :], dW["conv_q_pt"])
        ld(c_cw["Wk"][:, :], dW["conv_k_pt"])
        ld(c_cw["Wv"][:, :], dW["conv_v_pt"])
        ld(c_fs[:, :], dW["fir_s_pt"])
        ld(c_fl[:, :], dW["fir_l_pt"])
        ld(c_b1[:, :], dW["fus_b1_pt"])
        ld(c_b2[:, :], dW["fus_b2_c"])
        ld(c_fb[:, :], dW["floor_b_c"])
        ld(dump[0:1, 0:128], dW["norm_pf_r"])
        ld(dump[0:1, 128:256], dW["norm_o_r"])
        ld(cacc[0:1, :, :].rearrange("p a b -> p (a b)"), dW["gamma_r"])
        ld(r_wr[:, :, :], dW["w_routerT"].rearrange("(f p) e -> p f e", p=128))
        ldw(r_wb[:, :, :], dW["Wb"].rearrange("(f p) e -> p f e", p=128))
        ldw(r_wf[:, :, :], dW["floor_W"].rearrange("(f p) e -> p f e", p=128))
        ldw(r_w2[:, :, :], dW["fus_W2"].rearrange("(f p) e -> p f e", p=128))
        w_pre = [p.now(sX), p.now(sWA)]

        def POOL(fn, waits=()):
            return p.op("pool", fn, waits=waits, inc=sPO)

        POOL(lambda e: e.memset(c_id16[:, :], 0.0))
        POOL(lambda e: e.affine_select(
            out=c_id16[:, :], in_=c_id16[:, :], compare_op=ALU.not_equal,
            fill=1.0, base=0, pattern=[[-1, 128]], channel_multiplier=1))
        POOL(lambda e: e.memset(c_idf[:, :], 0.0))
        POOL(lambda e: e.affine_select(
            out=c_idf[:, :], in_=c_idf[:, :], compare_op=ALU.not_equal,
            fill=1.0, base=0, pattern=[[-1, 128]], channel_multiplier=1))
        POOL(lambda e: e.memset(c_mslu[:, :, :], 0.0))
        POOL(lambda e: e.affine_select(
            out=c_mslu[:, 0, :], in_=c_mslu[:, 0, :], compare_op=ALU.is_ge,
            fill=-1.0, base=0, pattern=[[1, 128]], channel_multiplier=-1))
        POOL(lambda e: e.affine_select(
            out=c_mslu[:, 1, :], in_=c_mslu[:, 1, :], compare_op=ALU.is_ge,
            fill=-1.0, base=0, pattern=[[-1, 128]], channel_multiplier=1))
        POOL(lambda e: e.memset(c_miu[:, :], 0.0))
        POOL(lambda e: e.affine_select(
            out=c_miu[:, :], in_=c_miu[:, :], compare_op=ALU.is_gt,
            fill=1.0, base=0, pattern=[[-1, 128]], channel_multiplier=1))
        POOL(lambda e: e.memset(c_ones[:, :], 1.0))
        POOL(lambda e: e.memset(v_fir[:, :, :], 0.0))
        POOL(lambda e: e.memset(raw[:, :, :], 0.0))
        POOL(lambda e: e.memset(tails[:, :, :, :], 0.0))
        POOL(lambda e: e.memset(S32[:, :, :], 0.0))
        w_msk = POOL(lambda e: e.memset(S16[:, :, :], 0.0))
        for h in range(NH):
            ctx[("S16", h)] = w_msk
            ctx[("S32", h)] = w_msk

        def MM(out_ap, lhsT, rhs, waits=(), start=True, stop=True):
            return p.op("pe", lambda e: e.matmul(out_ap, lhsT, rhs,
                                                 start=start, stop=stop),
                        waits=waits, inc=sPE if stop else None)

        def PET(out_ap, in_ap, waits=(), f32=False):
            kp = in_ap.shape[0]
            ident = (c_idf if f32 else c_id16)[0:kp, 0:kp]
            return p.op("pe", lambda e: e.transpose(out_ap, in_ap, ident),
                        waits=waits, inc=sPE)

        def ACT(out_ap, in_ap, func=AF.Copy, waits=(), bias=0.0, scale=1.0,
                accum=None):
            if accum is not None:
                return p.op("act", lambda e: e.activation(
                    out_ap, in_ap, func, bias=bias, scale=scale,
                    accum_out=accum), waits=waits, inc=sAC)
            return p.op("act", lambda e: e.activation(
                out_ap, in_ap, func, bias=bias, scale=scale),
                waits=waits, inc=sAC)

        def DVE(fn, waits=()):
            return p.op("dve", fn, waits=waits, inc=sDV)

        def bcast(dst, src_row, n):
            for j in range(0, n, 512):
                w = min(512, n - j)
                wm = MM(pb[5][:, 0:w], c_ones[:, :], src_row[:, j:j + w],
                        waits=w_pre + [w_msk, bankw(5)])
                ev = DVE(lambda e, j=j, w=w, d=dst: e.tensor_copy(
                    d[:, j:j + w], pb[5][:, 0:w]), waits=[wm])
                bankr(5, ev)
        bcast(c_pfb, dump[0:1, 0:128], 128)
        bcast(c_nob, dump[0:1, 128:256], 128)
        bcast(c_gmb, cacc[0:1, :, :].rearrange("p a b -> p (a b)"), H)
        w_const = [p.now(sDV), w_msk, p.now(sX), p.now(sWA)]

        cv_ev = None
        for e_ in range(NE):
            wu_src = dW["w_upT"][e_].rearrange("(f p) n -> p f n", p=128)
            wd_src = dW["w_downT"][e_].rearrange("(f p) n -> p f n", p=128)
            for j in range(4):
                si = j % 2
                lw = ldw(slot[si][:, :, :],
                         wu_src[:, :, j * 1024:(j + 1) * 1024],
                         waits=[ctx.get(("slotfree", si))])
                cv_ev = st_dma(scr_up[e_][:, :, j * 1024:(j + 1) * 1024],
                               slot[si][:, :, :], waits=[lw])
                ctx[("slotfree", si)] = cv_ev
            for j in range(2):
                si = j % 2
                lw = ldw(slot[si].rearrange("p a b -> p (a b)")
                         .rearrange("p (a b) -> p a b", a=16)[:, :, 0:512],
                         wd_src[:, :, j * 512:(j + 1) * 512],
                         waits=[ctx.get(("slotfree", si))])
                cv_ev = st_dma(scr_dn[e_][:, :, j * 512:(j + 1) * 512],
                               slot[si].rearrange("p a b -> p (a b)")
                               .rearrange("p (a b) -> p a b", a=16)[:, :, 0:512],
                               waits=[lw])
                ctx[("slotfree", si)] = cv_ev

        AUNITS = {"Wq": 0, "Wk": 1, "Wv": 0, "W1a": 1, "W1b": 0, "Wo": 1}

        def aunit_src(nm):
            if nm == "W1a":
                return dW["fus_W1"].rearrange("(f p) n -> p f n", p=128)[:, :, 0:1024]
            if nm == "W1b":
                return dW["fus_W1"].rearrange("(f p) n -> p f n", p=128)[:, :, 1024:2048]
            return dW[nm].rearrange("(f p) n -> p f n", p=128)

        def emit_dbg(g):
            if not dbg_keys or g != dbg_group or ctx.get("dbg_done"):
                return
            ctx["dbg_done"] = True
            bufs = dict(q_fm=q_fm, k_fm=k_fm, v_fir=v_fir, xn_fm=xn_fm,
                        raw=raw, cacc=cacc,
                        xr_fm=xr_fm, beta_b=beta_b, coef=coef, w8=w8,
                        fir_s=fir_s, fir_l=fir_l, o2_fm=o2_fm, S32=S32,
                        hid=hid, acc=acc, fus_fm=fus_fm, bf_fm=bf_fm,
                        d_T=d_T, d_uw=d_uw, d_AAT=d_AAT, d_fm3=d_fm3,
                        d_tok=d_tok, xr_fm32=xr_fm32)
            wd = [p.now(sPE), p.now(sDV), p.now(sAC), p.now(sPO)]
            from concourse.ap import AP as _AP
            for kk in dbg_keys:
                b = bufs[kk]
                ap = b if isinstance(b, _AP) else b.ap()
                flat = ap
                while len(flat.shape) > 2:
                    flat = flat.rearrange(
                        "p " + " ".join(chr(97 + i) for i in range(len(flat.shape) - 1))
                        + " -> p (" + " ".join(chr(97 + i) for i in range(len(flat.shape) - 1)) + ")")
                dd = nc.dram_tensor(f"dbg_{kk}", list(flat.shape),
                                    flat.dtype, kind="ExternalOutput").ap()
                st_dma(dd, flat, waits=wd)

        for g in range(NGRP):
            gis = g % GPS
            t0 = g * G
            gbar = [p.now(sPE), p.now(sDV), p.now(sAC)]
            chunk_store = [None] * CPG
            attn_store = [None] * CPG

            if gis == 0 and g > 0:
                ev = POOL(lambda e: e.memset(S32[:, :, :], 0.0), waits=gbar)
                ev = POOL(lambda e: e.memset(S16[:, :, :], 0.0), waits=[ev])
                for h in range(NH):
                    ctx[("S16", h)] = ev
                    ctx[("S32", h)] = ev
                ev2 = POOL(lambda e: e.memset(v_fir[:, :, 0:32], 0.0),
                           waits=gbar)
                ctx["vtail"] = ev2
                ev3 = POOL(lambda e: e.memset(tails[:, :, :, :], 0.0),
                           waits=gbar)
                ctx["ctail"] = ev3

            for c in range(CPG):
                tok = t0 + c * 128
                xb = c % 2
                st_ = stat[:, xb, :]
                wx = ld(x_t[:, 0, :], dx[tok:tok + 128, :],
                        waits=w_const + [ctx.get(("x_free", xb))])
                w_sum = DVE(lambda e, xb=xb, st_=st_: e.reduce_sum(
                    st_[:, 0:1], x_t[:, 0, :], axis=AX.X),
                    waits=[wx, ctx.get(("stat_free", xb))])
                w_sq = ACT(dump[:, :], x_t[:, 0, :], AF.Square,
                           waits=[wx, ctx.get(("stat_free_a", xb))],
                           accum=st_[:, 1:2])
                e1 = DVE(lambda e, st_=st_: e.tensor_scalar(
                    st_[:, 2:3], st_[:, 0:1], 1.0 / H, None, ALU.mult),
                    waits=[w_sum])
                e2 = DVE(lambda e, st_=st_: e.tensor_scalar(
                    st_[:, 3:4], st_[:, 1:2], 1.0 / H, None, ALU.mult),
                    waits=[w_sq])
                e3 = DVE(lambda e, st_=st_: e.tensor_tensor(
                    st_[:, 4:5], st_[:, 2:3], st_[:, 2:3], ALU.mult),
                    waits=[e1])
                e4 = DVE(lambda e, st_=st_: e.tensor_tensor(
                    st_[:, 5:6], st_[:, 3:4], st_[:, 4:5], ALU.subtract),
                    waits=[e2, e3])
                e5 = DVE(lambda e, st_=st_: e.tensor_scalar(
                    st_[:, 6:7], st_[:, 5:6], LN_EPS, None, ALU.add),
                    waits=[e4])
                e6 = ACT(st_[:, 7:8], st_[:, 6:7], AF.Ln, waits=[e5])
                e7 = ACT(st_[:, 8:9], st_[:, 7:8], AF.Exp, waits=[e6],
                         scale=-0.5)
                w_xn = DVE(lambda e, xb=xb, st_=st_: e.tensor_scalar(
                    xn_tok[:, 0, :], x_t[:, 0, :], st_[:, 2:3], st_[:, 8:9],
                    ALU.subtract, ALU.mult),
                    waits=[e7, ctx.get(("xn_free", xb))])
                w_op = DVE(lambda e, xb=xb: e.tensor_add(
                    outp[:, xb, :], x_t[:, 0, :], xn_tok[:, 0, :]),
                    waits=[w_xn, ctx.get(("outp_free", xb))])
                ctx[("x_free", xb)] = w_op
                st = st_dma(dout[tok:tok + 128, :], outp[:, xb, :],
                            waits=[w_op])
                chunk_store[c] = st
                ctx[("outp_free", xb)] = st

                w_sq2 = ACT(dump[:, :], xn_tok[:, 0, :], AF.Square,
                            waits=[w_xn], accum=st_[:, 9:10])
                e8 = DVE(lambda e, st_=st_: e.tensor_scalar(
                    st_[:, 10:11], st_[:, 9:10], 1.0 / H, FFN_EPS,
                    ALU.mult, ALU.add), waits=[w_sq2])
                e9 = ACT(st_[:, 11:12], st_[:, 10:11], AF.Ln, waits=[e8])
                eA = ACT(st_[:, 12:13], st_[:, 11:12], AF.Exp, waits=[e9],
                         scale=-0.5)
                ctx[("stat_free", xb)] = eA
                wxr1 = DVE(lambda e, xb=xb, st_=st_: e.tensor_scalar(
                    xr_tok[:, 0, :], xn_tok[:, 0, :], st_[:, 12:13], None,
                    ALU.mult), waits=[eA, ctx.get(("xr_free", xb))])
                ctx[("stat_free_a", xb)] = wxr1
                wxr = DVE(lambda e, xb=xb: e.tensor_mul(
                    xr_tok[:, 0, :], xr_tok[:, 0, :], c_gmb[:, :]),
                    waits=[wxr1])

                for bk in range(2):
                    tr = None
                    for q4 in range(4):
                        f = bk * 4 + q4
                        tr = PET(pb[3 + bk][:, ts(q4)], xr_tok[:, 0, ts(f)],
                                 waits=[wxr, bankw(3 + bk)], f32=True)
                    ev = ACT(xr_fm32.rearrange("p a b -> p (a b)")
                             [:, bk * 512:(bk + 1) * 512],
                             pb[3 + bk][:, :], waits=[tr])
                    bankr(3 + bk, ev)
                w_xr32 = ctx[("bk", 4)]
                ctx[("xr_free", xb)] = w_xr32
                w_xr16 = DVE(lambda e, c=c: e.tensor_copy(
                    xr_fm[:, :, c * 128:(c + 1) * 128],
                    xr_fm32[:, :, :]), waits=[w_xr32] + gbar)
                ctx[("xr16", c)] = w_xr16

                wrm = None
                for kf in range(8):
                    wrm = MM(pb[2][0:NE, 0:128], r_wr[:, kf, :],
                             xr_fm32[:, kf, :],
                             waits=[w_xr32, bankw(2)] + w_pre,
                             start=(kf == 0), stop=(kf == 7))
                wse = ACT(sc_fm[:, :], pb[2][0:NE, 0:128], waits=[wrm])
                bankr(2, wse)
                wst = PET(pb[2][:, 128:128 + NE], sc_fm[:, :],
                          waits=[wse, bankw(2)], f32=True)
                wsc = ACT(ext[:, 0:NE], pb[2][:, 128:128 + NE],
                          waits=[wst, ctx.get("ext_free_a")])
                bankr(2, wsc)
                wm1 = DVE(lambda e: e.reduce_max(
                    ext[:, 8:9], ext[:, 0:NE], axis=AX.X), waits=[wsc])
                weq1 = DVE(lambda e: e.tensor_scalar(
                    ext[:, 16:16 + NE], ext[:, 0:NE], ext[:, 8:9], None,
                    ALU.is_equal), waits=[wm1])
                wmsk = DVE(lambda e: e.scalar_tensor_tensor(
                    ext[:, 24:24 + NE], ext[:, 16:16 + NE], -1e30,
                    ext[:, 0:NE], ALU.mult, ALU.add), waits=[weq1])
                wm2 = DVE(lambda e: e.reduce_max(
                    ext[:, 9:10], ext[:, 24:24 + NE], axis=AX.X), waits=[wmsk])
                wd_ = DVE(lambda e: e.tensor_tensor(
                    ext[:, 10:11], ext[:, 8:9], ext[:, 9:10], ALU.subtract),
                    waits=[wm2])
                ww1 = ACT(ext[:, 11:12], ext[:, 10:11], AF.Sigmoid, waits=[wd_])
                ww2 = DVE(lambda e: e.tensor_scalar(
                    ext[:, 12:13], ext[:, 11:12], -1.0, 1.0, ALU.mult,
                    ALU.add), waits=[ww1])
                weq2 = DVE(lambda e: e.tensor_scalar(
                    ext[:, 32:32 + NE], ext[:, 0:NE], ext[:, 9:10], None,
                    ALU.is_equal), waits=[wm2])
                wnq1 = DVE(lambda e: e.tensor_scalar(
                    ext[:, 40:40 + NE], ext[:, 16:16 + NE], -1.0, 1.0,
                    ALU.mult, ALU.add), waits=[weq1])
                weq2b = DVE(lambda e: e.tensor_tensor(
                    ext[:, 32:32 + NE], ext[:, 32:32 + NE], ext[:, 40:40 + NE],
                    ALU.mult), waits=[weq2, wnq1])
                wpa = DVE(lambda e: e.tensor_scalar(
                    ext[:, 48:48 + NE], ext[:, 16:16 + NE], ext[:, 11:12],
                    None, ALU.mult), waits=[ww2])
                ww8 = DVE(lambda e, c=c: e.scalar_tensor_tensor(
                    w8[:, c, :], ext[:, 32:32 + NE], ext[:, 12:13],
                    ext[:, 48:48 + NE], ALU.mult, ALU.add),
                    waits=[weq2b, wpa, ctx.get(("w8_free", c))] + gbar)
                ctx[("w8", c)] = ww8
                ctx["ext_free_a"] = ww8

                for bk in range(2):
                    tr = None
                    for q4 in range(4):
                        f = bk * 4 + q4
                        tr = PET(pb[3 + bk][:, ts(q4)], xn_tok[:, 0, ts(f)],
                                 waits=[w_xn, bankw(3 + bk)], f32=True)
                    ev = ACT(xn_fm[:, bk * 4:(bk + 1) * 4, c * 128:(c + 1) * 128],
                             pb[3 + bk][:, :], waits=[tr] + gbar)
                    bankr(3 + bk, ev)
                ctx[("xn_free", xb)] = ctx[("bk", 4)]
                ctx[("xnT", c)] = ctx[("bk", 4)]

            w_xn_all = [ctx[("xnT", cc)] for cc in range(CPG)]
            if upto == "ln":
                emit_dbg(g)
                continue

            slot_ready = {}

            def load_unit(nm):
                si = AUNITS[nm]
                ev = ldw(slot[si][:, :, :], aunit_src(nm),
                         waits=[ctx.get(("slotfree", si))])
                slot_ready[nm] = (si, ev)

            load_unit("Wq")
            load_unit("Wk")
            ev_silu = {}

            def proj_stream(nm, dst_fm, fir_dst=None):
                si, wrdy = slot_ready[nm]
                ui = {"Wq": 0, "Wk": 1, "Wv": 2}[nm]
                evac_evs = []
                for fb in range(8):
                    pbk = fb % 2
                    wm = None
                    for kf in range(8):
                        wm = MM(pb[pbk][:, :],
                                slot[si][:, kf, ts(fb)],
                                xn_fm[:, kf, :],
                                waits=w_xn_all + [wrdy, bankw(pbk)],
                                start=(kf == 0), stop=(kf == 7))
                    ev = ACT(raw[:, fb, 4:4 + G], pb[pbk][:, :],
                             waits=[wm, ctx.get("raw_free"),
                                    ctx.get("ctail")])
                    bankr(pbk, ev)
                    evac_evs.append(ev)
                wtl = DVE(lambda e, ui=ui: e.tensor_copy(
                    raw[:, :, 1:4], tails[:, ui, :, 1:4]),
                    waits=evac_evs + [ctx.get("ctail"), ctx.get("raw_free")])
                wsv = DVE(lambda e, ui=ui: e.tensor_copy(
                    tails[:, ui, :, 1:4], raw[:, :, 1 + G:4 + G]),
                    waits=evac_evs + [wtl])
                chain = [None] * 8
                cw = c_cw[nm]
                silu_list = []
                for pr in range(4):
                    for i_tap in range(4):
                        for fb in (2 * pr, 2 * pr + 1):
                            ca = cacc[:, fb % 2, :]
                            win = raw[:, fb, 4 - i_tap: 4 - i_tap + G]
                            ct = 3 - i_tap
                            col = cw[:, fb * 4 + ct: fb * 4 + ct + 1]
                            if i_tap == 0:
                                chain[fb] = DVE(
                                    lambda e, ca=ca, win=win, col=col:
                                    e.tensor_scalar(ca, win, col, None,
                                                    ALU.mult),
                                    waits=[wtl, wsv,
                                           ctx.get(("cacc_free", fb % 2)),
                                           evac_evs[fb]])
                            else:
                                chain[fb] = DVE(
                                    lambda e, ca=ca, win=win, col=col:
                                    e.scalar_tensor_tensor(ca, win, col, ca,
                                                           ALU.mult, ALU.add),
                                    waits=[chain[fb]])
                    for fb in (2 * pr, 2 * pr + 1):
                        ca = cacc[:, fb % 2, :]
                        if fir_dst is None:
                            ev = ACT(dst_fm[:, fb, :], ca, AF.Silu,
                                     waits=[chain[fb], ctx.get("arena_free")])
                        else:
                            ev = ACT(fir_dst[:, fb, 32:32 + G], ca, AF.Silu,
                                     waits=[chain[fb], ctx.get("vtail")])
                        ctx[("cacc_free", fb % 2)] = ev
                        ev_silu[(nm, fb)] = ev
                        silu_list.append(ev)
                return silu_list, p.now(sDV)

            sl_q, cd = proj_stream("Wq", q_fm)
            ctx["raw_free"] = cd
            ctx[("slotfree", 0)] = sl_q[-1]
            load_unit("Wv")
            sl_k, cd = proj_stream("Wk", k_fm)
            ctx["raw_free"] = cd
            ctx[("slotfree", 1)] = sl_k[-1]
            load_unit("W1a")
            sl_v, cd = proj_stream("Wv", None, fir_dst=v_fir)
            ctx["raw_free"] = cd
            ctx[("slotfree", 0)] = sl_v[-1]
            load_unit("W1b")

            for which in ("b", "f"):
                rw = r_wb if which == "b" else r_wf
                row = 0 if which == "b" else 1
                wm = None
                for kf in range(8):
                    wm = MM(pb[2][0:NH, 0:G], rw[:, kf, :],
                            xn_fm[:, kf, :],
                            waits=w_xn_all + [bankw(2)] + w_pre,
                            start=(kf == 0), stop=(kf == 7))
                if which == "b":
                    ev = ACT(bf_fm[:, row, :], pb[2][0:NH, 0:G], AF.Sigmoid,
                             waits=[wm, ctx.get("bffm_free")])
                else:
                    ev = ACT(bf_fm[:, row, :], pb[2][0:NH, 0:G], AF.Sigmoid,
                             waits=[wm, ctx.get("bffm_free")],
                             bias=c_fb[:, 0:1])
                bankr(2, ev)
                ctx[("bf", row)] = ev

            for half, nm in [(0, "W1a"), (1, "W1b")]:
                si, wrdy = slot_ready[nm]
                for fbh in range(8):
                    fbg = half * 8 + fbh
                    pbk = fbg % 2
                    wm = None
                    for kf in range(8):
                        wm = MM(pb[pbk][:, :],
                                slot[si][:, kf, ts(fbh)],
                                xn_fm[:, kf, :],
                                waits=w_xn_all + [wrdy, bankw(pbk)],
                                start=(kf == 0), stop=(kf == 7))
                    ev = ACT(hid[:, fbg, :], pb[pbk][:, :], AF.Gelu,
                             waits=[wm, ctx.get("hid_free")],
                             bias=c_b1[:, fbg:fbg + 1])
                    bankr(pbk, ev)
                ctx[("slotfree", si)] = ev
                if nm == "W1b":
                    load_unit("Wo")
            w_hid = [p.now(sAC)]
            wm = None
            for kf in range(16):
                wm = MM(pb[2][0:32, 0:G], r_w2[:, kf, :], hid[:, kf, :],
                        waits=w_hid + [bankw(2)] + w_pre,
                        start=(kf == 0), stop=(kf == 15))
            wf2 = DVE(lambda e: e.tensor_scalar(
                fus_fm[:, :], pb[2][0:32, 0:G], c_b2[:, 0:1], None, ALU.add),
                waits=[wm])
            bankr(2, wf2)
            w_fus = wf2
            ctx["hid_free_attn"] = p.now(sPE)

            w_a1dv = p.now(sDV)
            for c in range(CPG):
                tb1 = PET(pt16[0][:, 0:NH], bf_fm[:, 0, c * 128:(c + 1) * 128],
                          waits=[ctx[("bf", 0)], bankw("t0")])
                tb2 = PET(pt16[0][:, 128:128 + NH],
                          bf_fm[:, 1, c * 128:(c + 1) * 128],
                          waits=[ctx[("bf", 1)]])
                wbe = ACT(beta_b[:, c, :], pt16[0][:, 0:NH],
                          waits=[tb2, ctx.get(("beta_free", c))] + gbar)
                wfe = ACT(ext[:, 0:NH], pt16[0][:, 128:128 + NH],
                          waits=[tb2, w_a1dv, ctx.get("ext_free")])
                bankr("t0", wfe)
                wb3 = PET(pb[2][:, 256:256 + 32],
                          fus_fm[:, c * 128:(c + 1) * 128],
                          waits=[w_fus, bankw(2)], f32=True)
                wfu = ACT(fus_tok[:, :], pb[2][:, 256:256 + 32],
                          waits=[wb3, ctx.get("fus_tok_free")])
                bankr(2, wfu)
                ctx[("betaT", c)] = wbe
                s0 = fus_tok[:, 0:32:4]
                s1 = fus_tok[:, 1:32:4]
                s2 = fus_tok[:, 2:32:4]
                s3 = fus_tok[:, 3:32:4]
                wmx = DVE(lambda e, s0=s0, s1=s1: e.tensor_tensor(
                    ext[:, 8:16], s0, s1, ALU.max), waits=[wfu, w_a1dv])
                wmx = DVE(lambda e, s2=s2: e.tensor_tensor(
                    ext[:, 8:16], ext[:, 8:16], s2, ALU.max), waits=[wmx])
                wd0 = DVE(lambda e, s0=s0: e.tensor_tensor(
                    ext[:, 16:24], s0, ext[:, 8:16], ALU.subtract), waits=[wmx])
                wd1 = DVE(lambda e, s1=s1: e.tensor_tensor(
                    ext[:, 24:32], s1, ext[:, 8:16], ALU.subtract), waits=[wmx])
                wd2 = DVE(lambda e, s2=s2: e.tensor_tensor(
                    ext[:, 32:40], s2, ext[:, 8:16], ALU.subtract), waits=[wmx])
                wex = ACT(ext[:, 16:40], ext[:, 16:40], AF.Exp,
                          waits=[wd0, wd1, wd2])
                wsm = DVE(lambda e: e.tensor_tensor(
                    ext[:, 40:48], ext[:, 16:24], ext[:, 24:32], ALU.add),
                    waits=[wex])
                wsm = DVE(lambda e: e.tensor_tensor(
                    ext[:, 40:48], ext[:, 40:48], ext[:, 32:40], ALU.add),
                    waits=[wsm])
                wrc = DVE(lambda e: e.reciprocal(ext[:, 48:56], ext[:, 40:48]),
                          waits=[wsm])
                ws3 = ACT(ext[:, 56:64], s3, AF.Sigmoid, waits=[wfu])
                wfl = DVE(lambda e: e.tensor_scalar(
                    ext[:, 0:NH], ext[:, 0:NH], -0.2, 1.0, ALU.mult, ALU.add),
                    waits=[wfe])
                wpv = DVE(lambda e, c=c: e.tensor_tensor(
                    coef[:, c, 3, :], ext[:, 56:64], ext[:, 0:NH], ALU.mult),
                    waits=[ws3, wfl, ctx.get(("coef_free", c))] + gbar)
                wq1 = DVE(lambda e, c=c: e.tensor_scalar(
                    ext[:, 56:64], coef[:, c, 3, :], -1.0, 1.0, ALU.mult,
                    ALU.add), waits=[wpv])
                wcw = DVE(lambda e: e.tensor_tensor(
                    ext[:, 48:56], ext[:, 48:56], ext[:, 56:64], ALU.mult),
                    waits=[wq1, wrc])
                for j, sj in enumerate((slice(16, 24), slice(24, 32),
                                        slice(32, 40))):
                    DVE(lambda e, c=c, j=j, sj=sj: e.tensor_tensor(
                        coef[:, c, j, :], ext[:, sj], ext[:, 48:56], ALU.mult),
                        waits=[wcw])
                ctx[("coef", c)] = p.now(sDV)
                ctx["ext_free"] = p.now(sDV)
                ctx["fus_tok_free"] = wd2
            ctx["bffm_free"] = p.now(sPE)

            if upto == "a2":
                emit_dbg(g)
                continue

            si_wo, wrdy_wo = slot_ready["Wo"]
            fir_ready = {}
            base = 32
            for pr in range(4):
              for i_tap in range(33):
                for fb in (2 * pr, 2 * pr + 1):
                    ca = cacc[:, fb % 2, 0:G]
                    key = ("fchain", fb)
                    if i_tap == 0:
                        cs = c_fs[:, fb * 3 + 2: fb * 3 + 3]
                        win = v_fir[:, fb, base:base + G]
                        ctx[key] = DVE(
                            lambda e, ca=ca, win=win, cs=cs:
                            e.tensor_scalar(ca, win, cs, None, ALU.mult),
                            waits=[ev_silu[("Wv", fb)],
                                   ctx.get(("cacc_free", fb % 2)),
                                   ctx.get("fir_free")])
                    elif i_tap == 1:
                        cs = c_fs[:, fb * 3 + 1: fb * 3 + 2]
                        win = v_fir[:, fb, base - 1:base - 1 + G]
                        ctx[key] = DVE(
                            lambda e, ca=ca, win=win, cs=cs:
                            e.scalar_tensor_tensor(ca, win, cs, ca,
                                                   ALU.mult, ALU.add),
                            waits=[ctx[key]])
                    elif i_tap == 2:
                        cs = c_fs[:, fb * 3: fb * 3 + 1]
                        win = v_fir[:, fb, base - 2:base - 2 + G]
                        ctx[key] = DVE(
                            lambda e, ca=ca, win=win, cs=cs, fb=fb:
                            e.scalar_tensor_tensor(fir_s[:, fb, :], win, cs,
                                                   ca, ALU.mult, ALU.add),
                            waits=[ctx[key]])
                        cl = c_fl[:, fb * 31 + 30: fb * 31 + 31]
                        winl = v_fir[:, fb, base:base + G]
                        ctx[key] = DVE(
                            lambda e, ca=ca, winl=winl, cl=cl:
                            e.tensor_scalar(ca, winl, cl, None, ALU.mult),
                            waits=[ctx[key]])
                    else:
                        lt = i_tap - 2
                        clt = 30 - lt
                        cl = c_fl[:, fb * 31 + clt: fb * 31 + clt + 1]
                        win = v_fir[:, fb, base - lt: base - lt + G]
                        if lt < 30:
                            ctx[key] = DVE(
                                lambda e, ca=ca, win=win, cl=cl:
                                e.scalar_tensor_tensor(ca, win, cl, ca,
                                                       ALU.mult, ALU.add),
                                waits=[ctx[key]])
                        else:
                            ev = DVE(
                                lambda e, ca=ca, win=win, cl=cl, fb=fb:
                                e.scalar_tensor_tensor(fir_l[:, fb, :], win,
                                                       cl, ca, ALU.mult,
                                                       ALU.add),
                                waits=[ctx[key]])
                            ctx[key] = ev
                            ctx[("cacc_free", fb % 2)] = ev
                            fir_ready[fb] = ev

            for c in range(CPG):
                for h in range(NH):
                    emit_delta(p, ctx, locals(), c, h)
                emit_epilogue(p, ctx, locals(), c, g)
            ctx["fir_free"] = p.now(sPE)
            ctx[("slotfree", si_wo)] = p.now(sPE)

            ev = DVE(lambda e: e.tensor_copy(
                v_fir[:, :, 0:32], v_fir[:, :, G:G + 32]),
                waits=sl_v + [p.now(sPE)])
            ctx["vtail"] = ev
            ctx["ctail"] = p.now(sDV)
            ctx["arena_free"] = None

            if upto != "attn":
                emit_moe(p, ctx, locals(), g)

            emit_dbg(g)

        p.op("sync", lambda e: e.nop(),
             waits=[p.now(sOS), p.now(sX), p.now(sWA), p.now(sPE),
                    p.now(sDV), p.now(sAC), p.now(sPO)])

        sems = [stack.enter_context(nc.semaphore(n)) for n in p.names]
        with nc.Block() as block:
            block.tensor(lambda e: p.replay("pe", e, sems))
            block.vector(lambda e: p.replay("dve", e, sems))
            block.scalar(lambda e: p.replay("act", e, sems))
            block.gpsimd(lambda e: p.replay("pool", e, sems))
            block.sync(lambda e: p.replay("sync", e, sems))
    return nc


def emit_delta(p, ctx, L_, c, h):
    par = h % NI
    sPE, sDV, sAC = L_["sPE"], L_["sDV"], L_["sAC"]
    MM, PET, ACT, DVE = L_["MM"], L_["PET"], L_["ACT"], L_["DVE"]
    bankw, bankr = L_["bankw"], L_["bankr"]
    q_fm, k_fm, v_fir = L_["q_fm"], L_["k_fm"], L_["v_fir"]
    fir_s, fir_l = L_["fir_s"], L_["fir_l"]
    pb, pt16 = L_["pb"], L_["pt16"]
    d_tok, d_fm3 = L_["d_tok"], L_["d_fm3"]
    d_knt, d_kbt, d_vbt = L_["d_knt"], L_["d_kbt"], L_["d_vbt"]
    d_AAT, d_PP, d_T = L_["d_AAT"], L_["d_PP"], L_["d_T"]
    d_uw, d_alT, d_unew, d_o2 = L_["d_uw"], L_["d_alT"], L_["d_unew"], L_["d_o2"]
    d_ft, o_tmp, o1_tmp, o2_fm = L_["d_ft"], L_["o_tmp"], L_["o1_tmp"], L_["o2_fm"]
    dstat, dump = L_["dstat"], L_["dump"]
    S16, S32 = L_["S16"], L_["S32"]
    beta_b, coef = L_["beta_b"], L_["coef"]
    c_mslu, c_miu, c_id16, c_pfb, c_nob = (L_["c_mslu"], L_["c_miu"],
                                           L_["c_id16"], L_["c_pfb"],
                                           L_["c_nob"])
    ev_silu, fir_ready = L_["ev_silu"], L_["fir_ready"]
    tw = c * 128
    ptk = f"t{par}"
    CB, SB_, OB = par, 2 + par, 4 + par
    st = dstat[:, par, :]
    free = ctx.get(("dfree", par))

    t1 = None
    for j, src in enumerate((q_fm[:, h, tw:tw + 128], k_fm[:, h, tw:tw + 128],
                             v_fir[:, h, 32 + tw:32 + tw + 128])):
        t1 = PET(pt16[par][:, ts(j)], src,
                 waits=[ev_silu[("Wq", h)], ev_silu[("Wk", h)],
                        ev_silu[("Wv", h)], bankw(ptk), free])
    e1 = ACT(d_tok[:, par, :, :].rearrange("p a b -> p (a b)"),
             pt16[par][:, 0:384], waits=[t1, free])
    bankr(ptk, e1)
    sq = ACT(dump[:, 0:128], d_tok[:, par, 0, :], AF.Square, waits=[e1],
             accum=st[:, 0:1])
    sk = ACT(dump[:, 128:256], d_tok[:, par, 1, :], AF.Square, waits=[e1],
             accum=st[:, 1:2])
    ee = DVE(lambda e: e.tensor_scalar(st[:, 2:4], st[:, 0:2], L2_EPS, None,
                                       ALU.add), waits=[sq, sk])
    el = ACT(st[:, 4:6], st[:, 2:4], AF.Ln, waits=[ee])
    er = ACT(st[:, 6:8], st[:, 4:6], AF.Exp, waits=[el], scale=-0.5)
    bcol = beta_b[:, c, h:h + 1]
    ebeta = ctx[("betaT", c)]
    eq = DVE(lambda e: e.tensor_scalar(d_tok[:, par, 0, :], d_tok[:, par, 0, :],
                                       st[:, 6:7], None, ALU.mult),
             waits=[er])
    ek = DVE(lambda e: e.tensor_scalar(d_knt[:, par, :], d_tok[:, par, 1, :],
                                       st[:, 7:8], None, ALU.mult),
             waits=[er, free])
    ekb = DVE(lambda e: e.tensor_scalar(d_kbt[:, par, :], d_knt[:, par, :],
                                        bcol, None, ALU.mult),
              waits=[ek, ebeta, free])
    evb = DVE(lambda e: e.tensor_scalar(d_vbt[:, par, :], d_tok[:, par, 2, :],
                                        bcol, None, ALU.mult),
              waits=[e1, ebeta, free])
    t2 = None
    for j, src in enumerate((d_tok[:, par, 0, :], d_knt[:, par, :],
                             d_kbt[:, par, :])):
        t2 = PET(pt16[par][:, ts(j)], src, waits=[eq, ek, ekb, bankw(ptk)])
    e2 = ACT(d_fm3[:, par, :, :].rearrange("p a b -> p (a b)"),
             pt16[par][:, 0:384], waits=[t2, free])
    bankr(ptk, e2)
    qn_f, kn_f, kb_f = (d_fm3[:, par, 0, :], d_fm3[:, par, 1, :],
                        d_fm3[:, par, 2, :])
    a1 = MM(pb[CB][:, 0:128], kb_f, kn_f, waits=[e2, bankw(CB)])
    a2 = MM(pb[CB][:, 128:256], kn_f, kb_f, waits=[e2])
    eA = DVE(lambda e: e.tensor_mul(
        d_AAT[:, par, :, :].rearrange("p a b -> p (a b)"),
        pb[CB][:, 0:256], c_mslu[:, :, :].rearrange("p a b -> p (a b)")),
        waits=[a2, free])
    bankr(CB, eA)
    prevP, prevT = d_AAT[:, par, 0, :], d_AAT[:, par, 1, :]
    prev_ev = eA
    for lvl in range(5):
        m1 = MM(pb[CB][:, 0:128], prevT, prevP, waits=[prev_ev, bankw(CB)])
        m2 = MM(pb[CB][:, 128:256], prevP, prevT, waits=[prev_ev])
        pev = ACT(d_PP[:, par, lvl, :, :].rearrange("p a b -> p (a b)"),
                  pb[CB][:, 0:256], waits=[m2, free])
        bankr(CB, pev)
        prevP = d_PP[:, par, lvl, 0, :]
        prevT = d_PP[:, par, lvl, 1, :]
        prev_ev = pev
    m64 = MM(pb[CB][:, 256:384], d_PP[:, par, 4, 0, :], d_PP[:, par, 4, 1, :],
             waits=[prev_ev, bankw(CB)])
    eT = DVE(lambda e: e.tensor_add(d_T[:, par, 0, :], pb[CB][:, 256:384],
                                    c_id16[:, :]), waits=[m64, free])
    bankr(CB, eT)
    cur = 0
    for kk in [4, 3, 2, 1, 0, -1]:
        lhsT = d_AAT[:, par, 0, :] if kk < 0 else d_PP[:, par, kk, 0, :]
        mp = MM(pb[CB][:, 384:512], lhsT, d_T[:, par, cur, :],
                waits=[eT, bankw(CB)])
        eT = DVE(lambda e, cur=cur: e.tensor_add(
            d_T[:, par, 1 - cur, :], pb[CB][:, 384:512], d_T[:, par, cur, :]),
            waits=[mp])
        bankr(CB, eT)
        cur = 1 - cur
    Tf = d_T[:, par, cur, :]
    mu = MM(pb[SB_][:, 0:128], Tf, d_vbt[:, par, :],
            waits=[eT, evb, bankw(SB_)])
    mw = MM(pb[SB_][:, 128:256], d_kbt[:, par, :], Tf, waits=[eT, ekb])
    euw = ACT(d_uw[:, par, :, :].rearrange("p a b -> p (a b)"),
              pb[SB_][:, 0:256], waits=[mw, free])
    bankr(SB_, euw)
    mal = MM(pb[SB_][:, 256:384], kn_f, qn_f, waits=[e2, bankw(SB_)])
    eal = DVE(lambda e: e.tensor_mul(d_alT[:, par, :], pb[SB_][:, 256:384],
                                     c_miu[:, :]), waits=[mal, free])
    bankr(SB_, eal)
    mws = MM(pb[SB_][:, 384:512], d_uw[:, par, 1, :], S16[:, h, :],
             waits=[euw, ctx[("S16", h)], bankw(SB_)])
    eun = DVE(lambda e: e.tensor_tensor(d_unew[:, par, :], d_uw[:, par, 0, :],
                                        pb[SB_][:, 384:512], ALU.subtract),
              waits=[mws, free])
    bankr(SB_, eun)
    mo1 = MM(pb[OB][:, 0:128], qn_f, S16[:, h, :],
             waits=[e2, ctx[("S16", h)], bankw(OB)], start=True, stop=False)
    mo = MM(pb[OB][:, 0:128], d_alT[:, par, :], d_unew[:, par, :],
            waits=[eal, eun], start=False, stop=True)
    mds = MM(pb[SB_][:, 0:128], d_knt[:, par, :], d_unew[:, par, :],
             waits=[ek, eun, bankw(SB_)])
    es1 = DVE(lambda e: e.tensor_tensor(S32[:, h, :], S32[:, h, :],
                                        pb[SB_][:, 0:128], ALU.add),
              waits=[mds, ctx[("S32", h)]])
    es2 = DVE(lambda e: e.tensor_copy(S16[:, h, :], S32[:, h, :]),
              waits=[es1])
    ctx[("S32", h)] = es2
    ctx[("S16", h)] = es2
    bankr(SB_, es1)
    tf = PET(pt16[par][:, 0:128], fir_s[:, h, tw:tw + 128],
             waits=[fir_ready[h], bankw(ptk)])
    tf = PET(pt16[par][:, 128:256], fir_l[:, h, tw:tw + 128],
             waits=[fir_ready[h]])
    eft = ACT(d_ft[:, par, :, :].rearrange("p a b -> p (a b)"),
              pt16[par][:, 0:256], waits=[tf, free])
    bankr(ptk, eft)
    wcf = ctx[("coef", c)]
    cc0 = coef[:, c, 0, h:h + 1]
    cc1 = coef[:, c, 1, h:h + 1]
    cc2 = coef[:, c, 2, h:h + 1]
    ccp = coef[:, c, 3, h:h + 1]
    eo = DVE(lambda e: e.tensor_scalar(o_tmp[:, par, :], d_ft[:, par, 0, :],
                                       cc0, None, ALU.mult),
             waits=[eft, wcf, free])
    eo = DVE(lambda e: e.scalar_tensor_tensor(
        o_tmp[:, par, :], d_ft[:, par, 1, :], cc1, o_tmp[:, par, :],
        ALU.mult, ALU.add), waits=[eo])
    eo = DVE(lambda e: e.scalar_tensor_tensor(
        o_tmp[:, par, :], pb[OB][:, 0:128], cc2, o_tmp[:, par, :],
        ALU.mult, ALU.add), waits=[eo, mo])
    eo = DVE(lambda e: e.scalar_tensor_tensor(
        o_tmp[:, par, :], d_tok[:, par, 2, :], ccp, o_tmp[:, par, :],
        ALU.mult, ALU.add), waits=[eo])
    bankr(OB, eo)
    sq1 = ACT(dump[:, 256:384], o_tmp[:, par, :], AF.Square, waits=[eo],
              accum=st[:, 8:9])
    em = DVE(lambda e: e.tensor_scalar(st[:, 9:10], st[:, 8:9], 1.0 / 128,
                                       RMS_EPS, ALU.mult, ALU.add),
             waits=[sq1])
    el1 = ACT(st[:, 10:11], st[:, 9:10], AF.Ln, waits=[em])
    er1 = ACT(st[:, 11:12], st[:, 10:11], AF.Exp, waits=[el1], scale=-0.5)
    eo1 = DVE(lambda e: e.scalar_tensor_tensor(
        o1_tmp[:, par, :], o_tmp[:, par, :], st[:, 11:12], c_pfb[:, :],
        ALU.mult, ALU.mult), waits=[er1, free])
    sq2 = ACT(dump[:, 384:512], o1_tmp[:, par, :], AF.Square, waits=[eo1],
              accum=st[:, 12:13])
    em2 = DVE(lambda e: e.tensor_scalar(st[:, 13:14], st[:, 12:13], 1.0 / 128,
                                        RMS_EPS, ALU.mult, ALU.add),
              waits=[sq2])
    el2 = ACT(st[:, 14:15], st[:, 13:14], AF.Ln, waits=[em2])
    er2 = ACT(st[:, 15:16], st[:, 14:15], AF.Exp, waits=[el2], scale=-0.5)
    eo2 = DVE(lambda e: e.scalar_tensor_tensor(
        d_o2[:, par, :], o1_tmp[:, par, :], st[:, 15:16], c_nob[:, :],
        ALU.mult, ALU.mult), waits=[er2, free])
    to2 = PET(pt16[par][:, 0:128], d_o2[:, par, :], waits=[eo2, bankw(ptk)])
    ef = ACT(o2_fm[:, h, :], pt16[par][:, 0:128],
             waits=[to2, ctx.get(("o2_free", h))])
    bankr(ptk, ef)
    ctx[("o2", h)] = ef
    ctx[("dfree", par)] = ef


def emit_epilogue(p, ctx, L_, c, g):
    sPE, sDV, sOS, sX = L_["sPE"], L_["sDV"], L_["sOS"], L_["sX"]
    MM, DVE = L_["MM"], L_["DVE"]
    bankw, bankr = L_["bankw"], L_["bankr"]
    pb, o2_fm, outp = L_["pb"], L_["o2_fm"], L_["outp"]
    slot, dout = L_["slot"], L_["dout"]
    ld, st_dma = L_["ld"], L_["st_dma"]
    chunk_store, attn_store = L_["chunk_store"], L_["attn_store"]
    si_wo, wrdy_wo = L_["si_wo"], L_["wrdy_wo"]
    t0 = L_["t0"]
    tok = t0 + c * 128
    xb = c % 2
    o2evs = [ctx[("o2", h)] for h in range(NH)]
    wl = ld(outp[:, xb, :], dout[tok:tok + 128, :],
            waits=[chunk_store[c], ctx.get(("outp_free", xb))])
    adds = []
    for nh in range(2):
        bk = nh
        wm = None
        for kf in range(8):
            wm = MM(pb[bk][:, :], o2_fm[:, kf, :],
                    slot[si_wo][:, kf, nh * 512:(nh + 1) * 512],
                    waits=o2evs + [wrdy_wo, bankw(bk)],
                    start=(kf == 0), stop=(kf == 7))
        ev = DVE(lambda e, xb=xb, nh=nh, bk=bk: e.tensor_tensor(
            outp[:, xb, nh * 512:(nh + 1) * 512],
            outp[:, xb, nh * 512:(nh + 1) * 512],
            pb[bk][:, :], ALU.add), waits=[wm, wl])
        bankr(bk, ev)
        adds.append(ev)
    st = st_dma(dout[tok:tok + 128, :], outp[:, xb, :], waits=adds)
    attn_store[c] = st
    ctx[("outp_free", xb)] = st
    for h in range(NH):
        ctx[("o2_free", h)] = p.now(sPE)


def emit_moe(p, ctx, L_, g):
    sPE, sDV, sAC, sOS, sX, sWA = (L_["sPE"], L_["sDV"], L_["sAC"],
                                   L_["sOS"], L_["sX"], L_["sWA"])
    MM, ACT, DVE, ldw = L_["MM"], L_["ACT"], L_["DVE"], L_["ldw"]
    ld, st_dma = L_["ld"], L_["st_dma"]
    bankw, bankr = L_["bankw"], L_["bankr"]
    pb = L_["pb"]
    slot, wd_sl, hid, sg_, xr_fm = (L_["slot"], L_["wd_sl"], L_["hid"],
                                    L_["sg_"], L_["xr_fm"])
    acc, w8, outp, dout = L_["acc"], L_["w8"], L_["outp"], L_["dout"]
    scr_up, scr_dn = L_["scr_up"], L_["scr_dn"]
    attn_store = L_["attn_store"]
    t0 = L_["t0"]
    w_xr_all = [ctx[("xr16", cc)] for cc in range(CPG)]
    mbar = [p.now(sAC), p.now(sDV)]

    def up_load2(e_, sp):
        si = sp % 2
        l1 = ldw(slot[si][:, :, 0:512],
                 scr_up[e_][:, :, sp * 512:(sp + 1) * 512],
                 waits=[ctx.get(("slotfree", si))])
        l2 = ldw(slot[si][:, :, 512:1024],
                 scr_up[e_][:, :, 2048 + sp * 512: 2048 + (sp + 1) * 512],
                 waits=[])
        return l2

    ev_up = {}
    for e_ in range(NE):
        for sp in range(2):
            if (e_, sp) not in ev_up:
                ev_up[(e_, sp)] = up_load2(e_, sp)
        for sp in range(4):
            si = sp % 2
            for d128 in range(4):
                fbh = sp * 4 + d128
                bku = fbh % 2
                bkg = 2 + fbh % 2
                sgs = sg_[:, :]
                wmU = None
                for kf in range(8):
                    wmU = MM(pb[bku][:, :],
                             slot[si][:, kf, d128 * 128:(d128 + 1) * 128],
                             xr_fm[:, kf, :],
                             waits=w_xr_all + [ev_up[(e_, sp)], bankw(bku)],
                             start=(kf == 0), stop=(kf == 7))
                wmG = None
                for kf in range(8):
                    wmG = MM(pb[bkg][:, :],
                             slot[si][:, kf,
                                      512 + d128 * 128: 512 + (d128 + 1) * 128],
                             xr_fm[:, kf, :],
                             waits=w_xr_all + [ev_up[(e_, sp)], bankw(bkg)],
                             start=(kf == 0), stop=(kf == 7))
                esg = ACT(sg_[:, fbh % 2, :], pb[bkg][:, :], AF.Silu,
                          waits=[wmG, ctx.get(("sg_free", fbh % 2))] + mbar)
                bankr(bkg, esg)
                eh = DVE(lambda e, fbh=fbh, bku=bku: e.tensor_tensor(
                    hid[:, fbh, :], pb[bku][:, :], sg_[:, fbh % 2, :],
                    ALU.mult),
                    waits=[wmU, esg, ctx.get("moe_down_done"),
                           ctx.get("hid_free_attn")] + mbar)
                bankr(bku, eh)
                ctx[("sg_free", fbh % 2)] = eh
            ctx[("slotfree", si)] = p.now(sPE)
            nsp, ne_ = sp + 2, e_
            if nsp >= 4:
                nsp, ne_ = nsp - 4, e_ + 1
            if ne_ < NE and (ne_, nsp) not in ev_up:
                ev_up[(ne_, nsp)] = up_load2(ne_, nsp)
        w_h_all = [p.now(sDV)]
        for ds in range(2):
            ldn = ldw(wd_sl[:, :, :], scr_dn[e_][:, :, ds * 512:(ds + 1) * 512],
                      waits=[ctx.get("wd_free")])
            for c in range(CPG):
                bk = 4 + (c % 2)
                wm = None
                for kf in range(16):
                    wm = MM(pb[bk][:, :], hid[:, kf, c * 128:(c + 1) * 128],
                            wd_sl[:, kf, :],
                            waits=w_h_all + [ldn, bankw(bk)],
                            start=(kf == 0), stop=(kf == 15))
                wcol = w8[:, c, e_:e_ + 1]
                if e_ == 0:
                    ev = DVE(lambda e, c=c, ds=ds, bk=bk, wcol=wcol:
                             e.tensor_scalar(
                                 acc[:, c, ds * 512:(ds + 1) * 512],
                                 pb[bk][:, :], wcol, None, ALU.mult),
                             waits=[wm, ctx[("w8", c)],
                                    ctx.get(("acc_free", c))])
                else:
                    ev = DVE(lambda e, c=c, ds=ds, bk=bk, wcol=wcol:
                             e.scalar_tensor_tensor(
                                 acc[:, c, ds * 512:(ds + 1) * 512],
                                 pb[bk][:, :], wcol,
                                 acc[:, c, ds * 512:(ds + 1) * 512],
                                 ALU.mult, ALU.add),
                             waits=[wm, ctx[("w8", c)],
                                    ctx.get(("accw", c, ds))])
                ctx[("accw", c, ds)] = ev
                bankr(bk, ev)
            ctx["wd_free"] = p.now(sPE)
        ctx["moe_down_done"] = p.now(sPE)
    ctx["moe_up_done"] = p.now(sPE)
    ctx["hid_free"] = p.now(sPE)

    for c in range(CPG):
        tok = t0 + c * 128
        xb = c % 2
        wl = ld(outp[:, xb, :], dout[tok:tok + 128, :],
                waits=[attn_store[c], ctx.get(("outp_free", xb))])
        ev = DVE(lambda e, xb=xb, c=c: e.tensor_tensor(
            outp[:, xb, :], outp[:, xb, :], acc[:, c, :], ALU.add),
            waits=[wl, ctx[("accw", c, 0)], ctx[("accw", c, 1)]])
        ctx[("acc_free", c)] = ev
        st = st_dma(dout[tok:tok + 128, :], outp[:, xb, :], waits=[ev])
        ctx[("outp_free", xb)] = st
        ctx[("w8_free", c)] = ev
        ctx[("coef_free", c)] = p.now(sDV)
        ctx[("beta_free", c)] = p.now(sDV)
    ctx["arena_free"] = p.now(sDV)


def make_inputs(i):
    base = {
        "Wq": np.ascontiguousarray(i["L0_attn_Wq"], np.float32),
        "Wk": np.ascontiguousarray(i["L0_attn_Wk"], np.float32),
        "Wv": np.ascontiguousarray(i["L0_attn_Wv"], np.float32),
        "Wo": np.ascontiguousarray(i["L0_attn_Wo"], np.float32),
        "fus_W1": np.ascontiguousarray(i["L0_attn_fus_W1"], np.float32),
        "fus_W2": np.ascontiguousarray(i["L0_attn_fus_W2"], np.float32),
        "Wb": np.ascontiguousarray(i["L0_attn_Wb"], np.float32),
        "floor_W": np.ascontiguousarray(i["L0_attn_floor_W"], np.float32),
        "w_routerT": np.ascontiguousarray(
            np.asarray(i["L0_ffn_w_router"]).T, np.float32),
        "w_upT": np.ascontiguousarray(
            np.asarray(i["L0_ffn_w_up"]).transpose(0, 2, 1), np.float32),
        "w_downT": np.ascontiguousarray(
            np.asarray(i["L0_ffn_w_down"]).transpose(0, 2, 1), np.float32),
        "conv_q_pt": np.ascontiguousarray(
            np.asarray(i["L0_attn_conv_q_w"]).reshape(8, 128, 4)
            .transpose(1, 0, 2).reshape(128, 32), np.float32),
        "conv_k_pt": np.ascontiguousarray(
            np.asarray(i["L0_attn_conv_k_w"]).reshape(8, 128, 4)
            .transpose(1, 0, 2).reshape(128, 32), np.float32),
        "conv_v_pt": np.ascontiguousarray(
            np.asarray(i["L0_attn_conv_v_w"]).reshape(8, 128, 4)
            .transpose(1, 0, 2).reshape(128, 32), np.float32),
        "fir_s_pt": np.ascontiguousarray(
            np.asarray(i["L0_attn_fir_short_w"]).transpose(1, 0, 2)
            .reshape(128, 24), np.float32),
        "fir_l_pt": np.ascontiguousarray(
            np.asarray(i["L0_attn_fir_long_w"]).transpose(1, 0, 2)
            .reshape(128, 248), np.float32),
        "fus_b1_pt": np.ascontiguousarray(
            np.asarray(i["L0_attn_fus_b1"]).reshape(16, 128).T, np.float32),
        "fus_b2_c": np.ascontiguousarray(
            np.asarray(i["L0_attn_fus_b2"]).reshape(32, 1), np.float32),
        "floor_b_c": np.ascontiguousarray(
            np.asarray(i["L0_attn_floor_b"]).reshape(NH, 1), np.float32),
        "norm_pf_r": np.ascontiguousarray(
            np.asarray(i["L0_attn_norm_pf_w"]).reshape(1, 128), np.float32),
        "norm_o_r": np.ascontiguousarray(
            np.asarray(i["L0_attn_norm_o_w"]).reshape(1, 128), np.float32),
        "gamma_r": np.ascontiguousarray(
            np.asarray(i["L0_ffn_norm_scale"]).reshape(1, H), np.float32),
    }
    hs = np.ascontiguousarray(np.asarray(i["L0_attn_hidden_states"]),
                              dtype=np.float32)
    in_maps = []
    for c in range(NCORES):
        m = dict(base)
        m["x"] = np.ascontiguousarray(
            hs[c * B_CORE:(c + 1) * B_CORE].reshape(TOK, H))
        in_maps.append(m)
    return in_maps


def kernel(**inputs):
    nc = build()
    in_maps = make_inputs(inputs)
    res = run_bass_kernel_spmd(nc, in_maps, core_ids=list(range(NCORES)))
    out = np.concatenate([r["out"].reshape(B_CORE, L, H)
                          for r in res.results], axis=0)
    return out
